# version 24 (final)
# baseline (speedup 1.0000x reference)
import math
import numpy as np
import ml_dtypes
import concourse.bass as bass
import concourse.mybir as mybir
from concourse.alu_op_type import AluOpType as OP
from concourse.bass_utils import run_bass_kernel_spmd

AF = mybir.ActivationFunctionType
F32 = mybir.dt.float32
BF16 = mybir.dt.bfloat16

T = 2048
C = 2048
H = 16
D = 128
INTER = 8192
BPC = 2
NCORE = 8
CH = 512
NCHUNK = T // CH
NCI = 16
NJ = 64
EPS = 1e-5
ATT_SCALE = 1.0 / math.sqrt(D)
C_H1 = 1.0 / (math.sqrt(C) * 64.0)
C_HD = 1.0 / (math.sqrt(INTER) * 64.0)
INV_C = 1.0 / C


def _allowed(qb, kb):
    return (kb <= qb) and ((qb - kb <= 16) or (kb < 2) or (kb % 4 == 0))


def _runs_for_kj(kj):
    out, cur = [], None
    for qb in range(32):
        a0, a1 = _allowed(qb, 2 * kj), _allowed(qb, 2 * kj + 1)
        kind = 0 if (a0 and a1) else 1 if a0 else 2 if a1 else None
        if kind is None:
            if cur:
                out.append(cur)
                cur = None
            continue
        if cur and cur[2] == kind and cur[1] == qb * 64:
            cur = (cur[0], (qb + 1) * 64, kind)
        else:
            if cur:
                out.append(cur)
            cur = (qb * 64, (qb + 1) * 64, kind)
    if cur:
        out.append(cur)
    return out


_RUNS = [_runs_for_kj(kj) for kj in range(16)]
_INTERVAL = [(r[0][0], r[-1][1]) for r in _RUNS]
for _kj in range(16):
    for _a, _b in zip(_RUNS[_kj][:-1], _RUNS[_kj][1:]):
        assert _a[1] == _b[0]


def _hadamard128():
    h = np.array([[1.0]])
    while h.shape[0] < 128:
        h = np.block([[h, h], [h, -h]])
    return h


def build_program(debug=False, dbg_pass=0):
    nc = bass.Bass("TRN2", target_bir_lowering=False)

    xT = nc.dram_tensor("xT", [BPC, 128, NCI, T], F32, kind="ExternalInput").ap()
    outT = nc.dram_tensor("outT", [BPC, 128, NCI, T], F32, kind="ExternalOutput").ap()
    wqT = nc.dram_tensor("wqT", [C, C], F32, kind="ExternalInput").ap()
    wkT = nc.dram_tensor("wkT", [C, C], F32, kind="ExternalInput").ap()
    wvT = nc.dram_tensor("wvT", [C, C], F32, kind="ExternalInput").ap()
    woT = nc.dram_tensor("woT", [C, C], F32, kind="ExternalInput").ap()
    upT = nc.dram_tensor("upT", [C, INTER], F32, kind="ExternalInput").ap()
    gateT = nc.dram_tensor("gateT", [C, INTER], F32, kind="ExternalInput").ap()
    downT = nc.dram_tensor("downT", [INTER, C], F32, kind="ExternalInput").ap()
    su_ug = nc.dram_tensor("su_ug", [128, NCI], F32, kind="ExternalInput").ap()
    su_dp = nc.dram_tensor("su_dp", [128, NJ], F32, kind="ExternalInput").ap()
    wsc_up = nc.dram_tensor("wsc_up", [128, NJ], F32, kind="ExternalInput").ap()
    wsc_ga = nc.dram_tensor("wsc_ga", [128, NJ], F32, kind="ExternalInput").ap()
    wsc_dp = nc.dram_tensor("wsc_dp", [128, NCI], F32, kind="ExternalInput").ap()
    c_ident = nc.dram_tensor("c_ident", [128, 128], BF16, kind="ExternalInput").ap()
    c_h128 = nc.dram_tensor("c_h128", [128, 128], BF16, kind="ExternalInput").ap()
    c_pswap = nc.dram_tensor("c_pswap", [128, 128], BF16, kind="ExternalInput").ap()
    c_cos = nc.dram_tensor("c_cos", [128, T], BF16, kind="ExternalInput").ap()
    c_sin2 = nc.dram_tensor("c_sin2", [128, T], BF16, kind="ExternalInput").ap()
    ubf = nc.dram_tensor("ubf", [NJ, 128, NCI * 128], BF16, kind="Internal").ap()
    gbf = nc.dram_tensor("gbf", [NJ, 128, NCI * 128], BF16, kind="Internal").ap()
    dbf = nc.dram_tensor("dbf", [NCI, 128, NJ, 128], BF16, kind="Internal").ap()
    dkind = "ExternalOutput" if debug else "Internal"
    x1d = nc.dram_tensor("x1d", [BPC, 128, NCI, T], F32, kind=dkind).ap()
    x1fd = nc.dram_tensor("x1fd", [BPC, 128, NCI, T], BF16, kind=dkind).ap()
    if debug:
        d_hT = nc.dram_tensor("d_hT", [128, NCI * T], BF16, kind="ExternalOutput").ap()
        d_q = nc.dram_tensor("d_q", [128, T], BF16, kind="ExternalOutput").ap()
        d_k = nc.dram_tensor("d_k", [128, T], BF16, kind="ExternalOutput").ap()
        d_vn = nc.dram_tensor("d_vn", [128, NCI * 128], BF16, kind="ExternalOutput").ap()
        d_ctx = nc.dram_tensor("d_ctx", [128, NCI * T], BF16, kind="ExternalOutput").ap()
        d_ao = nc.dram_tensor("d_ao", [128, NCI * T], BF16, kind="ExternalOutput").ap()
        d_z = nc.dram_tensor("d_z", [128, NJ * CH], BF16, kind="ExternalOutput").ap()
        d_zmap = nc.dram_tensor("d_zmap", [1, NJ], F32, kind="ExternalOutput").ap()
        d_dn = nc.dram_tensor("d_dn", [128, NCI * CH], BF16, kind="ExternalOutput").ap()
        d_x1fc = nc.dram_tensor("d_x1fc", [128, NCI * CH], BF16, kind="ExternalOutput").ap()
        d_z0 = nc.dram_tensor("d_z0", [128, NJ * CH], BF16, kind="ExternalOutput").ap()

    from contextlib import ExitStack
    with ExitStack() as _stack:
        A1 = _stack.enter_context(nc.sbuf_tensor("A1", [128, 65536], BF16))
        A2 = _stack.enter_context(nc.sbuf_tensor("A2", [128, 16384], BF16))
        A3 = _stack.enter_context(nc.sbuf_tensor("A3", [128, 8192], BF16))
        A4 = _stack.enter_context(nc.sbuf_tensor("A4", [128, 8192], BF16))
        s_arep = _stack.enter_context(nc.sbuf_tensor("s_arep", [128, 512], BF16))
        s_brep = _stack.enter_context(nc.sbuf_tensor("s_brep", [128, 512], BF16))
        s_vec = _stack.enter_context(nc.sbuf_tensor("s_vec", [128, 1536], F32))
        s_invb_t = _stack.enter_context(nc.sbuf_tensor("s_invb_t", [1, 1024], BF16))
        k_ident = _stack.enter_context(nc.sbuf_tensor("k_ident", [128, 128], BF16))
        k_h128 = _stack.enter_context(nc.sbuf_tensor("k_h128", [128, 128], BF16))
        k_pswap = _stack.enter_context(nc.sbuf_tensor("k_pswap", [128, 128], BF16))
        k_hd = _stack.enter_context(nc.sbuf_tensor("k_hd", [128, 128], BF16))
        k_onescol = _stack.enter_context(nc.sbuf_tensor("k_onescol", [128, 1], BF16))
        k_onesrow_b = _stack.enter_context(nc.sbuf_tensor("k_onesrow_b", [1, 128], BF16))
        k_onesrow_f = _stack.enter_context(nc.sbuf_tensor("k_onesrow_f", [1, 128], F32))
        v_suug = _stack.enter_context(nc.sbuf_tensor("v_suug", [128, NCI], F32))
        v_sudp = _stack.enter_context(nc.sbuf_tensor("v_sudp", [128, NJ], F32))
        v_wscup = _stack.enter_context(nc.sbuf_tensor("v_wscup", [128, NJ], F32))
        v_wscga = _stack.enter_context(nc.sbuf_tensor("v_wscga", [128, NJ], F32))
        v_wscdp = _stack.enter_context(nc.sbuf_tensor("v_wscdp", [128, NCI], F32))
        v_wg64 = _stack.enter_context(nc.sbuf_tensor("v_wg64", [128, NJ], F32))
        v_dvec = _stack.enter_context(nc.sbuf_tensor("v_dvec", [128, NJ], F32))
        v_wdp64 = _stack.enter_context(nc.sbuf_tensor("v_wdp64", [128, NCI], F32))
        k_eps = _stack.enter_context(nc.sbuf_tensor("k_eps", [128, 1], F32))
        psA = _stack.enter_context(nc.psum_tensor("psA", [128, 2048], F32))
        psB = _stack.enter_context(nc.psum_tensor("psB", [128, 2048], F32))
        ST = _stack.enter_context(nc.semaphore("ST"))
        SV = _stack.enter_context(nc.semaphore("SV"))
        SS = _stack.enter_context(nc.semaphore("SS"))
        SG = _stack.enter_context(nc.semaphore("SG"))
        SY = _stack.enter_context(nc.semaphore("SY"))
        block = _stack.enter_context(nc.Block())

        class _Holder:
            __slots__ = ("incs",)

            def __init__(self):
                self.incs = []

            def then_inc(self, sem, n):
                self.incs.append((sem, n))
                return self

        class _Rec:
            def __init__(self, ops):
                self._ops = ops

            def __getattr__(self, name):
                ops = self._ops

                def call(*args, **kwargs):
                    h = _Holder()
                    ops.append((name, args, kwargs, h))
                    return h
                return call

        _oplists = {k: [] for k in ("t", "v", "s", "g", "y")}
        eng = {k: _Rec(_oplists[k]) for k in _oplists}

        sems = {"t": ST, "v": SV, "s": SS, "g": SG, "y": SY}
        cnt = {k: 0 for k in sems}
        waited = {}

        def inc(key, instr, by=1):
            instr.then_inc(sems[key], by)
            cnt[key] += by
            return cnt[key]

        def wait(ekey, skey, val=None):
            val = cnt[skey] if val is None else val
            assert val <= cnt[skey], (ekey, skey, val, cnt[skey])
            if val <= 0:
                return
            w = waited.get((ekey, skey), 0)
            if val > w:
                eng[ekey].wait_ge(sems[skey], val)
                waited[(ekey, skey)] = val

        def dma(qkey, out_ap, in_ap):
            return inc(qkey, eng[qkey].dma_start(out_ap, in_ap), 16)

        te, ve, se, ge, ye = eng["t"], eng["v"], eng["s"], eng["g"], eng["y"]

        hT = A1[:, 0:NCI * T]
        ctxT = A1[:, NCI * T:2 * NCI * T]
        z_ar = A1[:, 0:NJ * CH]
        x1f_c = A1[:, 32768:32768 + NCI * CH]
        dnout = A1[:, 40960:40960 + NCI * CH]
        x1L = A1[:, 49152:65536].bitcast(F32)
        xstr = A1[:, 49152:65536].bitcast(F32)
        x1buf = A1[:, 32768:49152].bitcast(F32)
        Pbuf = A1[:, 49152:49152 + NCI * CH]
        Qbuf = A1[:, 57344:57344 + NCI * CH]

        expT = [A2[:, 0:1024], A2[:, 1024:2048]]
        vTbuf = A2[:, 2048:4096]
        vnat = A2[:, 4096:6144]
        m1buf = A2[:, 6144:7168]
        invrep = A2[:, 7168:8192]
        xb2 = A2[:, 0:8192]
        sq2 = A2[:, 8192:16384]
        sgbuf = [A2[:, 0:512], A2[:, 512:1024]]
        zscr = [A2[:, 1024:1536], A2[:, 1536:2048]]
        zspare = A2[:, 2048:2560]
        Lt1 = A2[:, 2560:3072]
        t1s = A2[:, 8192:8704]
        s_invb = s_invb_t[:]
        saccb = A2[:, 10240:11264]

        slab = [A3[:, i * 2048:(i + 1) * 2048] for i in range(4)]
        ubuf = [A3[:, 0:2048], A3[:, 2048:4096]]
        gbuf = [A3[:, 4096:6144], A3[:, 6144:8192]]
        wdA = A3[:, 0:8192]
        cosT = A4[:, 0:2048]
        sinT2 = A4[:, 2048:4096]
        qTb = A4[:, 4096:6144]
        kTb = A4[:, 6144:8192]
        k_st1 = A4[:, 4096:6144]
        wdB = A4[:, 0:8192]

        sv_m = s_vec[0:1, 0:512]
        sv_e2 = s_vec[0:1, 512:1024]
        sv_t = s_vec[0:1, 1024:1536]
        psB_bf = psB[:].bitcast(BF16)

        dma("y", k_ident[:], c_ident)
        dma("y", k_h128[:], c_h128)
        dma("y", k_pswap[:], c_pswap)
        dma("y", cosT, c_cos)
        dma("y", sinT2, c_sin2)
        dma("y", v_suug[:], su_ug)
        dma("y", v_sudp[:], su_dp)
        dma("y", v_wscup[:], wsc_up)
        dma("y", v_wscga[:], wsc_ga)
        dma("y", v_wscdp[:], wsc_dp)
        wait("v", "y")
        ve.tensor_scalar(k_hd[:], k_h128[:], C_HD, None, OP.mult)
        ve.tensor_tensor(v_dvec[:], v_wscup[:], v_sudp[:], OP.mult)
        ve.tensor_scalar(v_dvec[:], v_dvec[:], 64.0, None, OP.mult)
        ve.tensor_scalar(v_wg64[:], v_wscga[:], 64.0, None, OP.mult)
        ve.tensor_scalar(v_wdp64[:], v_wscdp[:], 64.0, None, OP.mult)
        ve.memset(k_eps[:], EPS)
        ve.memset(k_onescol[:], 1.0)
        ve.memset(k_onesrow_b[:], 1.0)
        v_prelude = inc("v", ve.memset(k_onesrow_f[:], 1.0))

        slot_free = [("v", 0)] * 4
        slot_resv = [False] * 4
        nslot = [0]

        def get_slot(reserve=False):
            for _ in range(4):
                s = nslot[0] % 4
                nslot[0] += 1
                if not slot_resv[s]:
                    if reserve:
                        slot_resv[s] = True
                    return s
            raise RuntimeError("no free slab slot")

        def release_slot(s, ekey, cv):
            slot_resv[s] = False
            slot_free[s] = (ekey, cv)

        def slab_load(slot_i, wsrc, col):
            ek, cv = slot_free[slot_i]
            wait("g", ek, cv)
            return dma("g", slab[slot_i].rearrange("p (a m) -> p a m", a=NCI),
                       wsrc[:, col * 128:(col + 1) * 128]
                       .rearrange("(a p) m -> p a m", p=128))

        conv_jobs = []
        for j in range(NJ):
            conv_jobs.append(("u", j))
            conv_jobs.append(("g", j))
        for c in range(NCI):
            for piece in range(4):
                conv_jobs.append(("d", c, piece))
        conv_i = [0]

        def pump_conversions(k):
            for _ in range(k):
                i = conv_i[0]
                if i >= len(conv_jobs):
                    return
                conv_i[0] = i + 1
                sl = get_slot()
                job = conv_jobs[i]
                if job[0] in ("u", "g"):
                    src = upT if job[0] == "u" else gateT
                    dst = ubf if job[0] == "u" else gbf
                    j = job[1]
                    g_ld = slab_load(sl, src, j)
                    wait("g", "g", g_ld)
                    g_st = dma("g", dst[j], slab[sl][:])
                else:
                    _, c, piece = job
                    j0 = piece * 16
                    ek, cv = slot_free[sl]
                    wait("g", ek, cv)
                    g_ld = dma("g", slab[sl].rearrange("p (a m) -> p a m", a=16),
                               downT[j0 * 128:(j0 + 16) * 128, c * 128:(c + 1) * 128]
                               .rearrange("(a p) m -> p a m", p=128))
                    wait("g", "g", g_ld)
                    g_st = dma("g", dbf[c][:, j0:j0 + 16, :],
                               slab[sl].rearrange("p (a m) -> p a m", a=16))
                slot_free[sl] = ("g", g_st)

        warm_state = {"last_v": 0}

        def keep_warm():
            wait("t", "v", cnt["v"])
            te.matmul(psA[:, 1024:1088], k_h128[:], k_ident[:, 0:64],
                      start=True, stop=True)

        psb67_free = ["v", v_prelude]

        def ln_stats(rhs_tiles, sq_tiles):
            wait("t", psb67_free[0], psb67_free[1])
            for ci in range(NCI):
                te.matmul(psB[0:1, 1024:1536], k_onescol[:], rhs_tiles[ci],
                          start=(ci == 0), stop=(ci == NCI - 1))
            last = None
            for ci in range(NCI):
                last = te.matmul(psB[0:1, 1536:2048], k_onescol[:], sq_tiles[ci],
                                 start=(ci == 0), stop=(ci == NCI - 1))
            t_stats = inc("t", last)
            wait("v", "t", t_stats)
            ve.tensor_scalar(sv_m, psB[0:1, 1024:1536], INV_C, None, OP.mult)
            ve.tensor_scalar(sv_e2, psB[0:1, 1536:2048], INV_C, None, OP.mult)
            ve.tensor_tensor(sv_t, sv_m, sv_m, OP.mult)
            v_var = inc("v", ve.tensor_tensor(sv_e2, sv_e2, sv_t, OP.subtract))
            wait("s", "v", v_var)
            se.activation(sv_t, sv_e2, AF.Ln, bias=k_eps[0:1, 0:1])
            s_a = inc("s", se.activation(sv_t, sv_t, AF.Exp, scale=-0.5))
            wait("v", "s", s_a)
            ve.tensor_tensor(sv_m, sv_m, sv_t, OP.mult)
            v_b = inc("v", ve.tensor_scalar(sv_m, sv_m, -1.0, None, OP.mult))
            wait("t", "v", v_b)
            te.matmul(psB[:, 1024:1536], k_onesrow_f[:], sv_t, start=True, stop=True)
            t_rep = inc("t", te.matmul(psB[:, 1536:2048], k_onesrow_f[:], sv_m,
                                       start=True, stop=True))
            wait("v", "t", t_rep)
            ve.tensor_copy(s_arep[:], psB[:, 1024:1536])
            v_ab = inc("v", ve.tensor_copy(s_brep[:], psB[:, 1536:2048]))
            psb67_free[:] = ["v", v_ab]
            return v_ab

        xregion_free = ["v", v_prelude]
        psA_free = ["s", 0]
        psB45_free = ["v", v_prelude]
        v_ctx_last = 0
        s_ao_last = 0
        t_last_hT_read = 0

        for b in range(BPC):
            xregs = [xstr, x1buf]
            xreg_free = [list(xregion_free), list(xregion_free)]
            for ch in range(NCHUNK):
                t0 = ch * CH
                pp = ch % 2
                xr = xregs[pp]
                wait("y", xreg_free[pp][0], xreg_free[pp][1])
                y_x = dma("y", xr.rearrange("p (a m) -> p a m", a=NCI),
                          xT[b, :, :, t0:t0 + CH])
                wait("s", "y", y_x)
                s_xb = inc("s", se.activation(xb2, xr[:], AF.Copy))
                wait("v", "s", s_xb)
                v_sq = inc("v", ve.tensor_tensor(sq2, xb2, xb2, OP.mult))
                wait("t", "v", v_sq)
                v_ab = ln_stats(
                    [xb2[:, ci * CH:(ci + 1) * CH] for ci in range(NCI)],
                    [sq2[:, ci * CH:(ci + 1) * CH] for ci in range(NCI)])
                wait("v", "v", v_ab)
                last = None
                for ci in range(NCI):
                    xv = xr[:, ci * CH:(ci + 1) * CH]
                    ve.tensor_tensor(t1s, xv, s_arep[:], OP.mult)
                    last = ve.tensor_tensor(hT[:, ci * T + t0:ci * T + t0 + CH],
                                            t1s, s_brep[:], OP.add)
                    if ci % 3 == 2:
                        keep_warm()
                v_h = inc("v", last)
                xreg_free[pp] = ["v", v_h]
            xregion_free = ["v", cnt["v"]]
            if True:
                pass

            v_ln1_done = cnt["v"]
            if debug and b == 0:
                wait("y", "v", v_ln1_done)
                dma("y", d_hT, hT)

            slot_of = {}
            gp_slab = {}
            for nm, wsrc in (("q", wqT), ("k", wkT), ("v", wvT)):
                s0 = get_slot(reserve=True)
                slot_of[(0, nm)] = s0
                gp_slab[(0, nm)] = slab_load(s0, wsrc, 0)

            ebuf_free_t = [0, 0]
            ebuf_free_v = [0, 0]
            sc_bank_free_s = [0, 0]
            sc_rot = [0]
            m2_free_v = v_prelude
            s_lnexp_last = 0
            v_ir = v_prelude
            ebuf_i = 0
            m1_free_v = v_prelude
            invb_free_t = 0

            for h in range(H):
                wq_s = slab[slot_of[(h, "q")]]
                wk_s = slab[slot_of[(h, "k")]]
                wv_s = slab[slot_of[(h, "v")]]

                def qkv_mms(wslab, half, bank0, gwait):
                    wait("t", "g", gwait)
                    last = None
                    for ci in range(NCI):
                        for sub in range(2):
                            col = half * 1024 + sub * 512
                            o0 = bank0 * 512 + sub * 512
                            last = te.matmul(
                                psA[:, o0:o0 + 512],
                                wslab[:, ci * 128:(ci + 1) * 128],
                                hT[:, ci * T + col:ci * T + col + 512],
                                start=(ci == 0), stop=(ci == NCI - 1),
                                skip_group_check=True)
                    return inc("t", last)

                def swap_mms(src_bf, bank0):
                    last = None
                    for sub in range(2):
                        o0 = bank0 * 512 + sub * 512
                        last = te.matmul(psA[:, o0:o0 + 512], k_pswap[:],
                                         src_bf[:, sub * 512:sub * 512 + 512],
                                         start=True, stop=True)
                    return inc("t", last)

                wait("t", "v", v_ln1_done)
                wait("t", psA_free[0], psA_free[1])
                wait("t", "v", v_ir)
                t_q0 = qkv_mms(wq_s, 0, 0, gp_slab[(h, "q")])
                t_q1 = qkv_mms(wq_s, 1, 2, gp_slab[(h, "q")])
                wait("s", "t", t_q0)
                s_qr0 = inc("s", se.activation(qTb[:, 0:1024], psA[:, 0:1024], AF.Copy))
                wait("s", "t", t_q1)
                s_qr1 = inc("s", se.activation(qTb[:, 1024:2048], psA[:, 1024:2048], AF.Copy))
                wait("t", "s", s_qr0)
                t_k0 = qkv_mms(wk_s, 0, 0, gp_slab[(h, "k")])
                wait("t", "s", s_qr1)
                t_k1 = qkv_mms(wk_s, 1, 2, gp_slab[(h, "k")])
                wait("s", "t", t_k0)
                s_kr0 = inc("s", se.activation(kTb[:, 0:1024], psA[:, 0:1024], AF.Copy))
                wait("s", "t", t_k1)
                s_kr1 = inc("s", se.activation(kTb[:, 1024:2048], psA[:, 1024:2048], AF.Copy))
                wait("t", "s", s_kr0)
                t_v0 = qkv_mms(wv_s, 0, 0, gp_slab[(h, "v")])
                wait("t", "s", s_kr1)
                t_v1 = qkv_mms(wv_s, 1, 2, gp_slab[(h, "v")])
                t_last_hT_read = t_v1
                wait("s", "t", t_v0)
                wait("s", "v", m2_free_v)
                s_v0 = inc("s", se.activation(vTbuf[:, 0:1024], psA[:, 0:1024], AF.Copy))
                wait("s", "t", t_v1)
                s_v1 = inc("s", se.activation(vTbuf[:, 1024:2048], psA[:, 1024:2048], AF.Copy))
                release_slot(slot_of[(h, "q")], "t", t_q1)
                release_slot(slot_of[(h, "k")], "t", t_k1)
                release_slot(slot_of[(h, "v")], "t", t_v1)
                if h + 1 < H:
                    for nm, wsrc in (("q", wqT), ("k", wkT), ("v", wvT)):
                        s_ = get_slot(reserve=True)
                        slot_of[(h + 1, nm)] = s_
                        gp_slab[(h + 1, nm)] = slab_load(s_, wsrc, h + 1)

                wait("t", "s", s_v0)
                t_sw0 = swap_mms(qTb[:, 0:1024], 0)
                wait("t", "s", s_v1)
                t_sw1 = swap_mms(qTb[:, 1024:2048], 2)
                wait("t", psB45_free[0], psB45_free[1])
                last = None
                for ti in range(NCI):
                    last = te.transpose(psB_bf[:, ti * 128:(ti + 1) * 128],
                                        vTbuf[:, ti * 128:(ti + 1) * 128], k_ident[:])
                t_tr = inc("t", last)
                wait("v", "t", t_tr)
                v_vn = inc("v", ve.tensor_copy(vnat[:], psB_bf[:, 0:2048]))
                psB45_free = ["v", v_vn]
                sw_q0, sw_q1 = m1buf[:], invrep[:]
                sw_k0, sw_k1 = vTbuf[:, 0:1024], vTbuf[:, 1024:2048]
                wait("s", "t", t_sw0)
                wait("s", "v", m1_free_v)
                s_sw0 = inc("s", se.activation(sw_q0, psA[:, 0:1024], AF.Copy))
                wait("t", "s", s_sw0)
                t_kw0 = swap_mms(kTb[:, 0:1024], 0)
                wait("s", "t", t_sw1)
                wait("s", "v", v_ctx_last)
                s_sw1 = inc("s", se.activation(sw_q1, psA[:, 1024:2048], AF.Copy))
                wait("t", "s", s_sw1)
                t_kw1 = swap_mms(kTb[:, 1024:2048], 2)
                wait("s", "t", max(t_kw0, t_tr))
                wait("s", "v", v_vn)
                s_kw0 = inc("s", se.activation(sw_k0, psA[:, 0:1024], AF.Copy))
                wait("s", "t", t_kw1)
                s_kw1 = inc("s", se.activation(sw_k1, psA[:, 1024:2048], AF.Copy))
                wait("v", "s", s_kw0)
                ve.tensor_tensor(kTb[:, 0:1024], kTb[:, 0:1024], cosT[:, 0:1024], OP.mult)
                ve.tensor_tensor(sw_k0, sw_k0, sinT2[:, 0:1024], OP.mult)
                v_k0 = inc("v", ve.tensor_tensor(kTb[:, 0:1024], kTb[:, 0:1024],
                                                 sw_k0, OP.add))
                wait("v", "s", s_kw1)
                ve.tensor_tensor(kTb[:, 1024:2048], kTb[:, 1024:2048],
                                 cosT[:, 1024:2048], OP.mult)
                ve.tensor_tensor(sw_k1, sw_k1, sinT2[:, 1024:2048], OP.mult)
                v_k1 = inc("v", ve.tensor_tensor(kTb[:, 1024:2048], kTb[:, 1024:2048],
                                                 sw_k1, OP.add))
                wait("v", "s", s_sw0)
                ve.tensor_tensor(qTb[:, 0:1024], qTb[:, 0:1024], cosT[:, 0:1024], OP.mult)
                ve.tensor_tensor(sw_q0, sw_q0, sinT2[:, 0:1024], OP.mult)
                v_q0 = inc("v", ve.tensor_tensor(qTb[:, 0:1024], qTb[:, 0:1024],
                                                 sw_q0, OP.add))
                wait("v", "s", s_sw1)
                ve.tensor_tensor(qTb[:, 1024:2048], qTb[:, 1024:2048],
                                 cosT[:, 1024:2048], OP.mult)
                ve.tensor_tensor(sw_q1, sw_q1, sinT2[:, 1024:2048], OP.mult)
                v_q1 = inc("v", ve.tensor_tensor(qTb[:, 1024:2048], qTb[:, 1024:2048],
                                                 sw_q1, OP.add))
                m1_free_v = v_q0
                m2_free_v = v_k1
                psA_free = ["s", s_kw1]

                if debug and b == 0 and h == 0:
                    wait("y", "v", v_q1)
                    wait("y", "v", v_k1)
                    wait("y", "v", v_vn)
                    dma("y", d_q, qTb)
                    dma("y", d_k, kTb)
                    dma("y", d_vn, vnat[:])
                    wait("t", "y", cnt["y"])
                pump_conversions(3)

                states = []
                for hf in range(2):
                    qlo_, qhi_ = hf * 1024, hf * 1024 + 1024
                    states.append(dict(
                        hf=hf, qlo=qlo_, qhi=qhi_,
                        ctx_first=[True, True], sums_first=[True, True],
                        t_av_last=0,
                        kjs=[kj for kj in range(16)
                             if _INTERVAL[kj][0] < qhi_ and _INTERVAL[kj][1] > qlo_]))

                def emit_scores(stt, kj):
                    nonlocal ebuf_i
                    qlo, qhi = stt["qlo"], stt["qhi"]
                    i0 = max(_INTERVAL[kj][0], qlo)
                    i1 = min(_INTERVAL[kj][1], qhi)
                    eb = expT[ebuf_i]
                    my_buf = ebuf_i
                    ebuf_i ^= 1
                    wait("v", "t", ebuf_free_t[my_buf])
                    v_ms = cnt["v"]
                    for (q0, q1, kind) in _RUNS[kj]:
                        a, bb = max(q0, qlo), min(q1, qhi)
                        if a >= bb or kind == 0:
                            continue
                        prng = slice(64, 128) if kind == 1 else slice(0, 64)
                        v_ms = inc("v", ve.memset(eb[prng, a - qlo:bb - qlo], 0.0))
                    s_exp_last = 0
                    w0 = (i0 // 512) * 512
                    while w0 < i1:
                        ww0, ww1 = max(w0, i0), min(w0 + 512, i1)
                        sbank = sc_rot[0]
                        wait("t", "s", sc_bank_free_s[sbank])
                        wait("t", "v", max(v_k1, v_q1))
                        wait("t", psB45_free[0], psB45_free[1])
                        t_sc = cnt["t"]
                        for (q0, q1, kind) in _RUNS[kj]:
                            a, bb = max(q0, ww0), min(q1, ww1)
                            if a >= bb:
                                continue
                            so = sbank * 512 + (a - ww0)
                            if kind == 0:
                                t_sc = inc("t", te.matmul(
                                    psB[:, so:so + (bb - a)],
                                    kTb[:, kj * 128:(kj + 1) * 128],
                                    qTb[:, a:bb], start=True, stop=True))
                            elif kind == 1:
                                t_sc = inc("t", te.matmul(
                                    psB[0:64, so:so + (bb - a)],
                                    kTb[:, kj * 128:kj * 128 + 64],
                                    qTb[:, a:bb], start=True, stop=True))
                            else:
                                t_sc = inc("t", te.matmul(
                                    psB[64:128, so:so + (bb - a)],
                                    kTb[:, kj * 128 + 64:(kj + 1) * 128],
                                    qTb[:, a:bb], start=True, stop=True))
                        wait("s", "t", t_sc)
                        wait("s", "t", ebuf_free_t[my_buf])
                        wait("s", "v", ebuf_free_v[my_buf])
                        s_exp_last = cnt["s"]
                        for (q0, q1, kind) in _RUNS[kj]:
                            a, bb = max(q0, ww0), min(q1, ww1)
                            if a >= bb:
                                continue
                            so = sbank * 512 + (a - ww0)
                            prng = (slice(0, 128) if kind == 0 else
                                    slice(0, 64) if kind == 1 else slice(64, 128))
                            s_exp_last = inc("s", se.activation(
                                eb[prng, a - qlo:bb - qlo],
                                psB[prng, so:so + (bb - a)],
                                AF.Exp, scale=ATT_SCALE))
                        sc_bank_free_s[sbank] = s_exp_last
                        sc_rot[0] ^= 1
                        w0 += 512
                    psB45_free[:] = ["s", s_exp_last]
                    return (kj, my_buf, eb, i0, i1, s_exp_last, v_ms)

                def emit_av(stt, st):
                    kj, my_buf, eb, i0, i1, s_exp_last, v_ms = st
                    qlo = stt["qlo"]
                    wait("t", "s", s_exp_last)
                    wait("t", "v", max(v_ms, v_vn))
                    wait("t", psb67_free[0], psb67_free[1])
                    wait("t", "v", v_ctx_last)
                    wait("t", "s", s_lnexp_last)
                    a = i0
                    while a < i1:
                        bb = min((a // 512 + 1) * 512, i1)
                        cb = (a - qlo) // 512
                        te.matmul(psB[:, 1024 + (a - qlo):1024 + (a - qlo) + (bb - a)],
                                  vnat[:, kj * 128:(kj + 1) * 128],
                                  eb[:, a - qlo:bb - qlo],
                                  start=stt["ctx_first"][cb], stop=False,
                                  skip_group_check=True)
                        stt["ctx_first"][cb] = False
                        stt["t_av_last"] = inc("t", te.matmul(
                            psA[0:1, (a - qlo):(a - qlo) + (bb - a)],
                            k_onescol[:], eb[:, a - qlo:bb - qlo],
                            start=stt["sums_first"][cb], stop=False,
                            skip_group_check=True))
                        stt["sums_first"][cb] = False
                        a = bb
                    ebuf_free_t[my_buf] = stt["t_av_last"]

                def emit_tail(stt):
                    nonlocal s_lnexp_last, v_ir, v_ctx_last, invb_free_t
                    qlo, qhi = stt["qlo"], stt["qhi"]
                    wait("s", "t", stt["t_av_last"])
                    wait("s", "t", invb_free_t)
                    se.activation(s_vec[0:1, 0:1024], psA[0:1, 0:1024], AF.Ln)
                    s_lnexp_last = inc("s", se.activation(s_invb, s_vec[0:1, 0:1024],
                                                          AF.Exp, scale=-1.0))
                    wait("t", "s", s_lnexp_last)
                    wait("t", "v", v_ir)
                    te.matmul(psA[:, 1024:1536], k_onesrow_b[:], s_invb[0:1, 0:512],
                              start=True, stop=True)
                    t_rep = inc("t", te.matmul(psA[:, 1536:2048], k_onesrow_b[:],
                                               s_invb[0:1, 512:1024], start=True, stop=True))
                    invb_free_t = t_rep
                    wait("v", "t", t_rep)
                    ve.tensor_copy(invrep[:, 0:512], psA[:, 1024:1536])
                    v_ir = inc("v", ve.tensor_copy(invrep[:, 512:1024], psA[:, 1536:2048]))
                    v_ctx_last = inc("v", ve.tensor_tensor(
                        ctxT[:, h * T + qlo:h * T + qhi],
                        psB[:, 1024:2048], invrep[:], OP.mult))
                    psb67_free[:] = ["v", v_ctx_last]

                seq = [(stt, kj) for stt in states for kj in stt["kjs"]]
                prev = None
                for stt, kj in seq:
                    st_sc = emit_scores(stt, kj)
                    if prev is not None:
                        pstt, pst = prev
                        emit_av(pstt, pst)
                        if pst[0] == pstt["kjs"][-1]:
                            emit_tail(pstt)
                    prev = (stt, st_sc)
                pstt, pst = prev
                emit_av(pstt, pst)
                emit_tail(pstt)
                psA_free = ["s", s_lnexp_last]


            if debug and b == 0:
                wait("y", "v", v_ctx_last)
                dma("y", d_ctx, ctxT)
            gp_wo = {}
            for co in range(2):
                s0 = get_slot(reserve=True)
                gp_wo[co] = (s0, slab_load(s0, woT, co))
            for co in range(NCI):
                sl, gpw = gp_wo[co]
                wait("t", "g", gpw)
                wait("t", "v", v_ctx_last)
                if co % 2 == 0:
                    wait("t", psA_free[0], psA_free[1])
                    ps = psA
                else:
                    wait("t", psB45_free[0], psB45_free[1])
                    wait("t", psb67_free[0], psb67_free[1])
                    ps = psB
                last = None
                for a in range(NCI):
                    for chv in range(4):
                        last = te.matmul(ps[:, chv * 512:(chv + 1) * 512],
                                         slab[sl][:, a * 128:(a + 1) * 128],
                                         ctxT[:, a * T + chv * 512:a * T + chv * 512 + 512],
                                         start=(a == 0), stop=(a == NCI - 1),
                                         skip_group_check=True)
                t_wo = inc("t", last)
                release_slot(sl, "t", t_wo)
                wait("s", "t", t_wo)
                s_ao = inc("s", se.activation(hT[:, co * T:(co + 1) * T], ps[:], AF.Copy))
                s_ao_last = s_ao
                if co % 2 == 0:
                    psA_free = ["s", s_ao]
                else:
                    psB45_free = ["s", s_ao]
                    psb67_free[:] = ["s", s_ao]
                if co + 2 < NCI:
                    s2_ = get_slot(reserve=True)
                    gp_wo[co + 2] = (s2_, slab_load(s2_, woT, co + 2))
                pump_conversions(2)

            if debug and b == 0:
                wait("y", "s", s_ao_last)
                dma("y", d_ao, hT)
            ao = hT
            wait("v", "t", cnt["t"])
            last = None
            for ci in range(NCI):
                last = ve.tensor_scalar(k_st1[:, ci * 128:(ci + 1) * 128], k_h128[:],
                                        v_suug[:, ci:ci + 1], C_H1, OP.mult, OP.mult)
            v_st1 = inc("v", last)
            RA = A1[:, 49152:65536].bitcast(F32)
            RB = A1[:, 32768:49152].bitcast(F32)
            pa_x = [RA, RB]
            pa_x1 = [RB, RA]
            pa_P = [A1[:, 49152:57344], A1[:, 32768:40960]]
            pa_Q = [A1[:, 57344:65536], A1[:, 40960:49152]]
            pa_free = [["s", s_ao_last], ["s", s_ao_last]]
            for ch in range(NCHUNK):
                t0 = ch * CH
                pp = ch % 2
                xr, x1r, Pb, Qb = pa_x[pp], pa_x1[pp], pa_P[pp], pa_Q[pp]
                wait("y", pa_free[pp][0], pa_free[pp][1])
                wait("y", "s", s_ao_last)
                y_x = dma("y", xr.rearrange("p (a m) -> p a m", a=NCI),
                          xT[b, :, :, t0:t0 + CH])
                ao_tiles = [ao[:, ci * T + t0:ci * T + t0 + CH] for ci in range(NCI)]
                wait("v", "s", s_ao_last)
                last = None
                for ci in range(NCI):
                    last = ve.tensor_tensor(sq2[:, ci * CH:(ci + 1) * CH],
                                            ao_tiles[ci], ao_tiles[ci], OP.mult)
                v_aosq = inc("v", last)
                keep_warm()
                wait("t", "v", v_aosq)
                v_ab = ln_stats(ao_tiles,
                                [sq2[:, ci * CH:(ci + 1) * CH] for ci in range(NCI)])
                wait("v", "y", y_x)
                wait("v", "y", cnt["y"])
                wait("v", "v", v_ab)
                last = None
                for ci in range(NCI):
                    ve.tensor_tensor(t1s, ao_tiles[ci], s_arep[:], OP.mult)
                    ve.tensor_tensor(t1s, t1s, s_brep[:], OP.add)
                    last = ve.tensor_tensor(x1r[:, ci * CH:(ci + 1) * CH],
                                            t1s, xr[:, ci * CH:(ci + 1) * CH], OP.add)
                    if ci % 2 == 1:
                        keep_warm()
                v_x1 = inc("v", last)
                wait("y", "v", v_x1)
                y_x1 = dma("y", x1d[b, :, :, t0:t0 + CH],
                           x1r.rearrange("p (a m) -> p a m", a=NCI))
                wait("s", "v", v_x1)
                s_xb = inc("s", se.activation(xb2, x1r[:], AF.Copy))
                wait("v", "s", s_xb)
                v_sq = inc("v", ve.tensor_tensor(sq2, xb2, xb2, OP.mult))
                wait("t", "v", v_sq)
                v_ab2 = ln_stats(
                    [xb2[:, ci * CH:(ci + 1) * CH] for ci in range(NCI)],
                    [sq2[:, ci * CH:(ci + 1) * CH] for ci in range(NCI)])
                wait("v", "v", v_ab2)
                last = None
                for ci in range(NCI):
                    ve.tensor_tensor(t1s, x1r[:, ci * CH:(ci + 1) * CH],
                                     s_arep[:], OP.mult)
                    last = ve.tensor_tensor(Pb[:, ci * CH:(ci + 1) * CH],
                                            t1s, s_brep[:], OP.add)
                    if ci % 3 == 2:
                        keep_warm()
                v_h2 = inc("v", last)
                wait("t", "v", v_h2)
                wait("t", psA_free[0], psA_free[1])
                rot_free = [0, 0]
                s_ev = 0
                for ci in range(NCI):
                    rt = ci % 2
                    wait("t", "s", rot_free[rt])
                    t_mm = inc("t", te.matmul(psA[:, rt * 512:(rt + 1) * 512],
                                              k_st1[:, ci * 128:(ci + 1) * 128],
                                              Pb[:, ci * CH:(ci + 1) * CH],
                                              start=True, stop=True))
                    wait("s", "t", t_mm)
                    s_ev = inc("s", se.activation(Qb[:, ci * CH:(ci + 1) * CH],
                                                  psA[:, rt * 512:(rt + 1) * 512], AF.Copy))
                    rot_free[rt] = s_ev
                psA_free = ["s", s_ev]
                wait("v", "s", s_ev)
                bufs = [Qb, Pb]
                cur = 0
                for s in range(4):
                    sbuf_src, dst = bufs[cur], bufs[cur ^ 1]
                    step = 1 << s
                    last = None
                    nn = 0
                    for lo in range(NCI):
                        if lo & step:
                            continue
                        hi = lo + step
                        a_ap = sbuf_src[:, lo * CH:(lo + 1) * CH]
                        b_ap = sbuf_src[:, hi * CH:(hi + 1) * CH]
                        ve.tensor_tensor(dst[:, lo * CH:(lo + 1) * CH], a_ap, b_ap, OP.add)
                        last = ve.tensor_tensor(dst[:, hi * CH:(hi + 1) * CH],
                                                a_ap, b_ap, OP.subtract)
                        nn += 1
                        if nn % 3 == 0:
                            keep_warm()
                    inc("v", last)
                    cur ^= 1
                v_x1f = cnt["v"]
                wait("y", "v", v_x1f)
                dma("y", x1fd[b, :, :, t0:t0 + CH],
                    bufs[cur].rearrange("p (a m) -> p a m", a=NCI))
                pa_free[pp] = ["v", v_x1f]
                pump_conversions(2)
            xregion_free = ["v", cnt["v"]]

        while conv_i[0] < len(conv_jobs):
            pump_conversions(8)
        g_conv_done = cnt["g"]

        x1f_free_t = 0
        z_free_t = 0
        dn_free_v = 0
        out_stored_y = 0
        up_pair_free_v = [cnt["v"], cnt["v"]]
        hd_rot_free_s = [0, 0]
        s_dn_prev = 0
        pending_L = None
        pending_store = None
        ubuf_free = [("s", 0), ("s", 0)]
        gbuf_free = [("s", 0), ("s", 0)]
        wd_free = [("v", 0), ("v", 0)]
        zscr_free_t = [0, 0]

        for p in range(BPC * NCHUNK):
            b = p // NCHUNK
            t0 = (p % NCHUNK) * CH
            first = (p == 0)
            if first:
                wait("y", "g", g_conv_done)
                wait("y", "v", cnt["v"])
                wait("y", "s", cnt["s"])
                wait("s", "v", cnt["v"])
                for s_ in range(4):
                    ek, cv = slot_free[s_]
                    wait("y", ek, cv)
            if p == 0:
                wait("y", "t", x1f_free_t)
                y_x1f = dma("y", x1f_c.rearrange("p (a m) -> p a m", a=NCI),
                            x1fd[b, :, :, t0:t0 + CH])
            else:
                y_x1f = y_x1f_next
            if debug and p == dbg_pass:
                dma("y", d_x1fc, x1f_c)
            gp_u = {}
            wait("y", "t", z_free_t)
            for j in range(2):
                ek, cv = ubuf_free[j]
                wait("y", ek, cv)
                gp_u[(j, "u")] = dma("y", ubuf[j][:], ubf[j])
                ek, cv = gbuf_free[j]
                wait("y", ek, cv)
                gp_u[(j, "g")] = dma("y", gbuf[j][:], gbf[j])

            zbufs = [z_ar[:, i * CH:(i + 1) * CH] for i in range(NJ)] + [zspare]
            slot = list(range(NJ))
            spare_sl = [NJ]
            done_ev = {}
            pending = [[(lo, lo + (1 << s)) for lo in range(NJ) if not (lo & (1 << s))]
                       for s in range(6)]

            rr = [0]

            def try_stages(drain=False):
                progress = True
                while progress:
                    progress = False
                    for s in range(6):
                        for pr in pending[s][:]:
                            lo, hi = pr
                            kl, kh = (s - 1, lo), (s - 1, hi)
                            if kl in done_ev and kh in done_ev:
                                ek = "v"
                                if drain:
                                    rr[0] = (rr[0] + 1) % 3
                                    ek = "g" if rr[0] == 0 else "v"
                                ee = eng[ek]
                                e1, c1 = done_ev[kl]
                                e2, c2 = done_ev[kh]
                                wait(ek, e1, c1)
                                wait(ek, e2, c2)
                                a_ap = zbufs[slot[lo]]
                                b_ap = zbufs[slot[hi]]
                                ee.tensor_tensor(zbufs[spare_sl[0]], a_ap, b_ap, OP.subtract)
                                vl = inc(ek, ee.tensor_tensor(a_ap, a_ap, b_ap, OP.add))
                                slot[hi], spare_sl[0] = spare_sl[0], slot[hi]
                                done_ev[(s, lo)] = done_ev[(s, hi)] = (ek, vl)
                                pending[s].remove(pr)
                                progress = True

            def emit_hd_z(j):
                jb = j % 2
                wait("t", "v", up_pair_free_v[jb])
                wait("t", "s", hd_rot_free_s[jb])
                if first and j < 2:
                    wait("t", psB45_free[0], psB45_free[1])
                t_hd = inc("t", te.matmul(psB[:, jb * 512:(jb + 1) * 512], k_hd[:],
                                          zscr[jb], start=True, stop=True))
                zscr_free_t[jb] = t_hd
                wait("s", "t", t_hd)
                if j == 0:
                    wait("s", "t", z_free_t)
                s_z = inc("s", se.activation(z_ar[:, j * CH:(j + 1) * CH],
                                             psB[:, jb * 512:(jb + 1) * 512], AF.Copy))
                hd_rot_free_s[jb] = s_z
                done_ev[(-1, j)] = ("s", s_z)
                try_stages()

            for j in range(NJ):
                jb = j % 2
                wait("t", "y", gp_u[(j, "g")])
                wait("t", "y", y_x1f)
                wait("t", "v", up_pair_free_v[jb])
                if j < 2:
                    if first:
                        wait("t", psA_free[0], psA_free[1])
                    wait("t", "s", s_dn_prev)
                base = jb * 1024
                last = None
                for ci in range(NCI):
                    last = te.matmul(psA[:, base:base + 512],
                                     ubuf[jb][:, ci * 128:(ci + 1) * 128],
                                     x1f_c[:, ci * CH:(ci + 1) * CH],
                                     start=(ci == 0), stop=(ci == NCI - 1))
                for ci in range(NCI):
                    last = te.matmul(psA[:, base + 512:base + 1024],
                                     gbuf[jb][:, ci * 128:(ci + 1) * 128],
                                     x1f_c[:, ci * CH:(ci + 1) * CH],
                                     start=(ci == 0), stop=(ci == NCI - 1))
                t_upg = inc("t", last)
                ubuf_free[jb] = ("t", t_upg)
                gbuf_free[jb] = ("t", t_upg)
                if j + 2 < NJ:
                    wait("y", "t", t_upg)
                    gp_u[(j + 2, "u")] = dma("y", ubuf[jb][:], ubf[j + 2])
                    gp_u[(j + 2, "g")] = dma("y", gbuf[jb][:], gbf[j + 2])
                if j >= 1:
                    emit_hd_z(j - 1)
                if j == 4 and pending_L is not None:
                    pending_store = pending_L()
                    pending_L = None
                if j == 16 and pending_store is not None:
                    pending_store()
                    pending_store = None
                wait("s", "t", t_upg)
                s_sg = inc("s", se.activation(sgbuf[jb], psA[:, base + 512:base + 1024],
                                              AF.Silu, scale=v_wg64[:, j:j + 1]))
                wait("v", "s", s_sg)
                wait("v", "t", zscr_free_t[jb])
                ve.tensor_tensor(zscr[jb], psA[:, base:base + 512], sgbuf[jb], OP.mult)
                v_z = inc("v", ve.tensor_scalar(zscr[jb], zscr[jb],
                                                v_dvec[:, j:j + 1], None, OP.mult))
                up_pair_free_v[jb] = v_z
            emit_hd_z(NJ - 1)
            x1f_free_t = cnt["t"]
            if p + 1 < BPC * NCHUNK:
                bn = (p + 1) // NCHUNK
                tn = ((p + 1) % NCHUNK) * CH
                wait("y", "t", x1f_free_t)
                y_x1f_next = dma("y", x1f_c.rearrange("p (a m) -> p a m", a=NCI),
                                 x1fd[bn, :, :, tn:tn + CH])
            while any(pending[s] for s in range(6)):
                before = sum(len(pending[s]) for s in range(6))
                try_stages(drain=True)
                keep_warm()
                after = sum(len(pending[s]) for s in range(6))
                assert after < before, "butterfly scheduler stuck"
            v_stages_done = cnt["v"]
            g_stages_done = cnt["g"]
            slot = slot
            if debug and p == dbg_pass:
                wait("y", "v", v_stages_done)
                dma("y", d_z, z_ar)
                wait("t", "y", cnt["y"])
            wd_slabs = [wdA, wdB]
            gp_d = {}
            for c in range(2):
                ek, cv = wd_free[c]
                wait("y", ek, cv)
                wait("y", "t", x1f_free_t)
                gp_d[c] = dma("y", wd_slabs[c].rearrange("p (a m) -> p a m", a=NJ), dbf[c])
            t_dn_last = 0
            s_dn_last = 0
            for c in range(NCI):
                cb = c % 2
                wait("t", "y", gp_d[c])
                wait("t", "v", v_stages_done)
                wait("t", "g", g_stages_done)
                wait("t", "s", s_dn_last if c >= 2 else 0)
                if c < 2:
                    wait("t", "v", up_pair_free_v[0])
                    wait("t", "v", up_pair_free_v[1])
                last = None
                for j in range(NJ):
                    last = te.matmul(psA[:, cb * 512:(cb + 1) * 512],
                                     wd_slabs[cb][:, j * 128:(j + 1) * 128],
                                     zbufs[slot[j]],
                                     start=(j == 0), stop=(j == NJ - 1))
                t_dn = inc("t", last)
                t_dn_last = t_dn
                wd_free[cb] = ("t", t_dn)
                if c + 2 < NCI:
                    wait("y", "t", t_dn)
                    gp_d[c + 2] = dma("y",
                                      wd_slabs[cb].rearrange("p (a m) -> p a m", a=NJ),
                                      dbf[c + 2])
                wait("s", "t", t_dn)
                if c == 0:
                    wait("s", "v", dn_free_v)
                s_dn = inc("s", se.activation(dnout[:, c * CH:(c + 1) * CH],
                                              psA[:, cb * 512:(cb + 1) * 512],
                                              AF.Copy, scale=v_wdp64[:, c:c + 1]))
                s_dn_last = s_dn
            if debug and p == dbg_pass:
                wait("y", "s", s_dn_last)
                dma("y", d_dn, dnout)
            z_free_t = t_dn_last
            s_dn_prev = s_dn_last

            def make_L(b, t0, s_dn_last):
                def emit_L():
                    nonlocal out_stored_y, dn_free_v
                    wait("y", "y", out_stored_y)
                    y_x1 = dma("y", x1L.rearrange("p (a m) -> p a m", a=NCI),
                               x1d[b, :, :, t0:t0 + CH])
                    dn_tiles = [dnout[:, ci * CH:(ci + 1) * CH] for ci in range(NCI)]
                    wait("v", "s", s_dn_last)
                    last = None
                    for ci in range(NCI):
                        last = ve.tensor_tensor(sq2[:, ci * CH:(ci + 1) * CH],
                                                dn_tiles[ci], dn_tiles[ci], OP.mult)
                    v_dnsq = inc("v", last)
                    wait("t", "v", v_dnsq)
                    v_ab = ln_stats(dn_tiles,
                                    [sq2[:, ci * CH:(ci + 1) * CH] for ci in range(NCI)])
                    wait("v", "y", y_x1)
                    wait("v", "v", v_ab)
                    last = None
                    for ci in range(NCI):
                        ve.tensor_tensor(x1L[:, ci * CH:(ci + 1) * CH],
                                         x1L[:, ci * CH:(ci + 1) * CH], s_brep[:], OP.add)
                        ve.tensor_tensor(Lt1, dn_tiles[ci], s_arep[:], OP.mult)
                        last = ve.tensor_tensor(x1L[:, ci * CH:(ci + 1) * CH],
                                                x1L[:, ci * CH:(ci + 1) * CH], Lt1, OP.add)
                    v_out = inc("v", last)
                    dn_free_v = v_out

                    def emit_store():
                        nonlocal out_stored_y
                        wait("y", "v", v_out)
                        out_stored_y = dma("y", outT[b, :, :, t0:t0 + CH],
                                           x1L.rearrange("p (a m) -> p a m", a=NCI))
                    return emit_store
                return emit_L

            pending_L = make_L(b, t0, s_dn_last)

        pending_L()()

        wait("y", "y", out_stored_y)
        wait("g", "g", cnt["g"])

        def _replay(e, oplist):
            for name, args, kwargs, h in oplist:
                instr = getattr(e, name)(*args, **kwargs)
                for sem, n in h.incs:
                    instr.then_inc(sem, n)

        @block.tensor
        def _(e):
            _replay(e, _oplists["t"])

        @block.vector
        def _(e):
            _replay(e, _oplists["v"])

        @block.scalar
        def _(e):
            _replay(e, _oplists["s"])

        @block.gpsimd
        def _(e):
            _replay(e, _oplists["g"])

        @block.sync
        def _(e):
            _replay(e, _oplists["y"])

    return nc



_NC_CACHE = {}


def _get_nc():
    if "nc" not in _NC_CACHE:
        _NC_CACHE["nc"] = build_program()
    return _NC_CACHE["nc"]


def _tile_major(v, ntiles):
    return np.ascontiguousarray(v.reshape(ntiles, 128).T.astype(np.float32))


def _rope_tables():
    t = np.arange(T, dtype=np.float64)
    inv_freq = 1.0 / 10000.0 ** (np.arange(0, D, 2, dtype=np.float64) / D)
    freqs = np.outer(t, inv_freq)
    emb = np.concatenate([freqs, freqs], 1)
    cosT = np.cos(emb).T
    sinT = np.sin(emb).T
    sinT2 = sinT.copy()
    sinT2[:64] *= -1.0
    return (cosT.astype(ml_dtypes.bfloat16), sinT2.astype(ml_dtypes.bfloat16))


def kernel(L0_attn_x, L0_attn_w_q, L0_attn_w_k, L0_attn_w_v, L0_attn_w_o,
           L0_ffn_SU_ug, L0_ffn_SU_dp, L0_ffn_Wscale_ug, L0_ffn_Wscale_dp,
           L0_ffn_W_up, L0_ffn_W_gate, L0_ffn_W_down):
    x = np.asarray(L0_attn_x, dtype=np.float32)
    nc = _get_nc()

    wq = np.ascontiguousarray(np.asarray(L0_attn_w_q, np.float32).T)
    wk = np.ascontiguousarray(np.asarray(L0_attn_w_k, np.float32).T)
    wv = np.ascontiguousarray(np.asarray(L0_attn_w_v, np.float32).T)
    wo = np.ascontiguousarray(np.asarray(L0_attn_w_o, np.float32).T)
    up = np.ascontiguousarray(np.asarray(L0_ffn_W_up, np.float32).T)
    gate = np.ascontiguousarray(np.asarray(L0_ffn_W_gate, np.float32).T)
    down = np.ascontiguousarray(np.asarray(L0_ffn_W_down, np.float32).T)
    wsc = np.asarray(L0_ffn_Wscale_ug, np.float32)
    consts = {
        "su_ug": _tile_major(np.asarray(L0_ffn_SU_ug, np.float32), NCI),
        "su_dp": _tile_major(np.asarray(L0_ffn_SU_dp, np.float32), NJ),
        "wsc_up": _tile_major(wsc[:INTER], NJ),
        "wsc_ga": _tile_major(wsc[INTER:], NJ),
        "wsc_dp": _tile_major(np.asarray(L0_ffn_Wscale_dp, np.float32), NCI),
        "c_ident": np.eye(128).astype(ml_dtypes.bfloat16),
        "c_h128": _hadamard128().astype(ml_dtypes.bfloat16),
    }
    pswap = np.zeros((128, 128))
    pswap[np.arange(128), (np.arange(128) + 64) % 128] = 1.0
    consts["c_pswap"] = pswap.astype(ml_dtypes.bfloat16)
    consts["c_cos"], consts["c_sin2"] = _rope_tables()

    in_maps = []
    for core in range(NCORE):
        xs = x[core * BPC:(core + 1) * BPC]
        xTc = np.ascontiguousarray(
            xs.transpose(0, 2, 1).reshape(BPC, NCI, 128, T).transpose(0, 2, 1, 3))
        m = {"xT": xTc, "wqT": wq, "wkT": wk, "wvT": wv, "woT": wo,
             "upT": up, "gateT": gate, "downT": down}
        m.update(consts)
        in_maps.append(m)

    res = run_bass_kernel_spmd(nc, in_maps, core_ids=list(range(NCORE)))
    outs = []
    for core in range(NCORE):
        o = res.results[core]["outT"]
        o = np.asarray(o).transpose(0, 2, 1, 3).reshape(BPC, C, T).transpose(0, 2, 1)
        outs.append(o)
    return np.ascontiguousarray(np.concatenate(outs, axis=0))
